# version 02 (final)
# baseline (speedup 1.0000x reference)
import os

import numpy as np

import concourse.bass as bass
import concourse.mybir as mybir
from concourse.bass_utils import run_bass_kernel_spmd

F32 = mybir.dt.float32
BF16 = mybir.dt.bfloat16
AX = mybir.AxisListType.X
AF = mybir.ActivationFunctionType
OP = mybir.AluOpType

D = 2048
S = 2048
NBATCH = 2
H = 16
KVC = 512
QC = 1024
FF = 8192
NE, NR = 8, 8
ER = NE * NR
NT = S // 128
DCH = D // 128
FCH = FF // 128
CCH = KVC // 128
QCH = QC // 128
GT = 512
NGRP = (NBATCH * S) // GT
EPS = 1e-5
GCA = 0.07135481627260226
GCB = 1.5957691216057308
NEG_BIG = -1.0e30

DEBUG_PHASE = os.environ.get("MK_PHASE", "")
DBG = os.environ.get("MK_DBG", "")


def host_prep(inp):
    f32 = np.float32
    c = {}

    def lay(a, fn=None):
        if fn is not None:
            a = fn(a)
        return np.ascontiguousarray(a, dtype=f32)

    c["wkvd"] = lay(inp["L0_attn_w_kv_down"],
                    lambda a: a.reshape(DCH, 128, KVC).transpose(1, 0, 2))
    c["wqd"] = lay(inp["L0_attn_w_q_down"],
                   lambda a: a.reshape(DCH, 128, QC).transpose(1, 0, 2))
    c["wupk"] = lay(inp["L0_attn_w_kv_up_k"],
                    lambda a: a.reshape(CCH, 128, H, 128).transpose(2, 1, 0, 3))
    c["wupv"] = lay(inp["L0_attn_w_kv_up_v"],
                    lambda a: a.reshape(CCH, 128, H, 128).transpose(2, 1, 0, 3))
    c["wqup"] = lay(inp["L0_attn_w_q_up"],
                    lambda a: a.reshape(QCH, 128, H, 128).transpose(2, 1, 0, 3))
    c["wropes"] = lay(np.concatenate([inp["L0_attn_w_k_rope"],
                                      inp["L0_attn_w_q_rope"]], axis=1),
                      lambda a: a.reshape(DCH, 128, 256).transpose(1, 0, 2))
    c["wout"] = lay(inp["L0_attn_w_out"],
                    lambda a: a.reshape(DCH, 128, D).transpose(1, 0, 2))
    c["w1p"] = lay(inp["L0_ffn_W1"],
                   lambda a: a.reshape(FCH, 128, DCH, 128).transpose(0, 3, 2, 1))
    c["w2p"] = lay(inp["L0_ffn_W2"], lambda a: a.T.reshape(FCH, 128, 4, 512))
    c["a1t"] = lay(inp["L0_ffn_A1"],
                   lambda a: a.reshape(ER, D).T.reshape(DCH, 128, ER)
                   .transpose(1, 0, 2))
    c["a2t"] = lay(inp["L0_ffn_A2"],
                   lambda a: a.reshape(ER, FF).T.reshape(FCH, 128, ER)
                   .transpose(1, 0, 2))
    c["b1rf"] = lay(inp["L0_ffn_B1"], lambda a: a.transpose(0, 2, 1).reshape(ER, FF))
    c["b2rf"] = lay(inp["L0_ffn_B2"], lambda a: a.transpose(0, 2, 1).reshape(ER, D))
    c["wrt"] = lay(inp["L0_ffn_W_r"],
                   lambda a: a.T.reshape(DCH, 128, NE).transpose(1, 0, 2))
    c["br"] = lay(inp["L0_ffn_b_r"], lambda a: a.reshape(1, NE))
    c["b1t"] = lay(inp["L0_ffn_b1"], lambda a: a.reshape(FCH, 128).T)
    c["b2"] = lay(inp["L0_ffn_b2"], lambda a: a.reshape(1, D))
    c["ident"] = np.eye(128, dtype=f32)
    inv = 10000.0 ** (-np.arange(0, 128, 2, dtype=np.float64) / 128)
    ang = np.arange(S, dtype=np.float64)[:, None] * inv[None, :]
    c["costab"] = lay(np.cos(ang).astype(f32),
                      lambda a: a.reshape(NT, 128, 64).transpose(1, 0, 2))
    c["sintab"] = lay(np.sin(ang).astype(f32),
                      lambda a: a.reshape(NT, 128, 64).transpose(1, 0, 2))
    return c


class P:
    def __init__(self, e, E, sems):
        self.e = e
        self.E = E
        self.sems = sems
        self.c = {k: 0 for k in sems}
        self._lw = {}

    def _wait(self, name, val):
        if val <= 0 or self._lw.get(name, -1) >= val:
            return
        self._lw[name] = val
        self.E.wait_ge(self.sems[name], val)

    def op(self, eng, fn, waits=(), inc=False):
        if inc:
            self.c[eng] += 1
        if self.e == eng:
            for (sn, v) in waits:
                if sn != eng:
                    self._wait(sn, v)
            ins = fn(self.E)
            if inc:
                ins.then_inc(self.sems[eng], 1)
        return self.c[eng]

    def dma(self, eng, sem, out, in_, waits=()):
        self.c[sem] += 16
        if self.e == eng:
            for (sn, v) in waits:
                self._wait(sn, v)
            self.E.dma_start(out=out, in_=in_).then_inc(self.sems[sem], 16)
        return self.c[sem]

    def drain(self, eng):
        if self.e == eng:
            self.E.drain()

    def wait_only(self, eng, waits):
        if self.e == eng:
            for (sn, v) in waits:
                self._wait(sn, v)

    def barrier(self):
        snap = dict(self.c)
        me = {"t": "t", "v": "v", "s": "s", "q": "q", "g": None}.get(self.e)
        for sn, v in snap.items():
            if sn != me:
                self._wait(sn, v)


class Toks(dict):
    def up(self, key, val):
        self[key] = max(self.get(key, 0), val)


def build_program():
    nc = bass.Bass("TRN2", target_bir_lowering=False)
    T = {}

    def din(name, shape):
        T[name] = nc.dram_tensor(name, list(shape), F32, kind="ExternalInput").ap()

    din("x", (NBATCH, S, D))
    for n, shp in [("wkvd", (128, DCH, KVC)), ("wqd", (128, DCH, QC)),
                   ("wupk", (H, 128, CCH, 128)), ("wupv", (H, 128, CCH, 128)),
                   ("wqup", (H, 128, QCH, 128)), ("wropes", (128, DCH, 256)),
                   ("wout", (128, DCH, D)),
                   ("w1p", (FCH, 128, DCH, 128)), ("w2p", (FCH, 128, 4, 512)),
                   ("a1t", (128, DCH, ER)), ("a2t", (128, FCH, ER)),
                   ("b1rf", (ER, FF)), ("b2rf", (ER, D)),
                   ("wrt", (128, DCH, NE)), ("br", (1, NE)),
                   ("b1t", (128, FCH)), ("b2", (1, D)),
                   ("ident", (128, 128)), ("costab", (128, NT, 64)),
                   ("sintab", (128, NT, 64))]:
        din(n, shp)
    T["out"] = nc.dram_tensor("out", [NBATCH, S, D], F32, kind="ExternalOutput").ap()
    if DBG:
        T["dbgb"] = nc.dram_tensor("dbgb", [4096, 8192], BF16,
                                   kind="ExternalOutput").ap()

    def scr(name, shape, dt):
        T[name] = nc.dram_tensor(name, list(shape), dt, kind="Internal").ap()

    scr("w1sc", (FCH, 128, D), BF16)
    scr("w2sc", (FCH, 128, 4, 512), BF16)
    scr("kvdsc", (128, DCH, KVC), BF16)
    scr("qdsc", (128, DCH, QC), BF16)
    scr("upksc", (H, 128, CCH, 128), BF16)
    scr("upvsc", (H, 128, CCH, 128), BF16)
    scr("qupsc", (H, 128, QCH, 128), BF16)
    scr("ropesc", (128, DCH, 256), BF16)
    scr("woutsc", (128, DCH, D), BF16)
    scr("x2sc", (NBATCH * S, D), F32)
    scr("ctxsc", (NBATCH, H, 128, S), BF16)
    scr("ssc", (NGRP, FCH, 128, 512), BF16)

    LIMIT = 205 * 1024
    arena = nc.alloc_sbuf_tensor("arena", [128, LIMIT], mybir.dt.uint8)
    ARENA_BASE = nc.lookup_mloc(arena).addr
    cur = [0]

    def sb(name, shape, dt, at=None):
        nb = 2 if dt == BF16 else 4
        n = 1
        for s_ in shape[1:]:
            n *= s_
        sz = (n * nb + 31) & ~31
        off = cur[0] if at is None else at
        assert off + sz <= LIMIT, f"SBUF overflow {name}: {(off + sz) // 1024}KB"
        T[name] = nc.alloc_sbuf_tensor_at(name, list(shape), dt,
                                          offset=ARENA_BASE + off)
        if at is None:
            cur[0] += sz
        return off + sz

    sb("ident_b", (128, 128), BF16)
    sb("ones_row", (1, 128), BF16)
    sb("ones_col", (128, 1), BF16)
    sb("eps_col", (128, 1), F32)
    sb("ln16_c", (1, 1), F32)
    sb("mb", (1, 1), BF16)
    sb("xf", (128, 2, D), F32)
    sb("ln_bf", (128, 2, D), BF16)
    sb("sqdump", (128, D), BF16)
    sb("lns", (128, 2, 8), F32)
    sb("pf32", (128, 2, D), F32)
    PERS = sb("msc", (128, 2, 8), F32)

    o = [PERS]

    def sbo(name, shape, dt):
        o[0] = sb(name, shape, dt, at=o[0])

    PBF_AT = o[0]
    sbo("pbf", (128, 2, D), BF16)
    o[0] = PBF_AT
    XTL_AT = o[0]
    sbo("xTl", (128, DCH, S), BF16)
    sbo("krT", (128, S), BF16)
    sbo("qrT", (128, S), BF16)
    sbo("latT", (128, CCH, S), BF16)
    sbo("qlatT", (128, QCH, S), BF16)
    sbo("rop_b", (128, DCH, 256), BF16)
    sbo("krx", (1, 672), F32)
    sbo("sqn", (128, D), BF16)
    sbo("cost_b", (128, NT, 64), F32)
    sbo("sint_b", (128, NT, 64), F32)
    AB_END = o[0]
    sbo("wdn", (128, DCH, 512), BF16)
    assert o[0] <= LIMIT

    o = [XTL_AT]
    sbo("kh", (128, 2, S), BF16)
    sbo("qh", (128, 2, S), BF16)
    sbo("vh", (128, 2, NT, 132), BF16)
    sbo("wkh", (128, 2, CCH, 128), BF16)
    sbo("wvh", (128, 2, CCH, 128), BF16)
    sbo("wqh", (128, 2, QCH, 128), BF16)
    sbo("negM", (128, 2), F32)
    sbo("attnT", (128, NT, 512), BF16)
    sbo("ctxs", (128, 2, 4, 128), BF16)
    assert o[0] <= XTL_AT + 64 * 1024, f"C overlay {o[0]//1024}KB"

    o = [PERS]
    sbo("woutb", (128, DCH, D), BF16)
    sbo("ctxt", (128, 2, H, 128), BF16)
    sbo("projf", (128, 2, D), F32)
    assert o[0] <= LIMIT

    o = [PERS]
    sbo("hT", (128, FCH // 2, 512), BF16)
    sbo("res", (128, 4, D), F32)
    sbo("xT", (128, DCH, 512), BF16)
    sbo("w1c", (128, 2, DCH, 128), BF16)
    sbo("w2c", (128, 2, 4, 512), BF16)
    sbo("sc", (128, 2, 4, 512), BF16)
    sbo("b1b", (ER, FF), BF16)
    sbo("b2b", (ER, D), BF16)
    sbo("a1b", (128, DCH, ER), BF16)
    sbo("a2b", (128, FCH, ER), BF16)
    sbo("wrb", (128, DCH, NE), BF16)
    sbo("brb", (1, NE), BF16)
    sbo("b1tb", (128, FCH), F32)
    sbo("b2row", (1, D), BF16)
    sbo("u1f", (ER, 512), F32)
    sbo("r8", (128, 16), F32)
    sbo("mk1", (128, NE), BF16)
    sbo("mk2", (128, NE), BF16)
    sbo("mk64", (128, ER), BF16)
    sbo("maskT", (ER, 2, 512), BF16)
    sbo("z1T", (ER, 2, 512), BF16)
    sbo("z2T", (ER, 512), BF16)
    sbo("wcol", (128, 2, 4), F32)
    sbo("cbuf", (128, 5, 512), BF16)
    sbo("gq", (128, 2, 512), BF16)
    sbo("gz", (128, 2, 512), BF16)
    assert o[0] <= LIMIT, f"FFN {o[0] // 1024}KB"

    for i, n in enumerate(["PB0", "PB1", "PB2", "PB3", "PB4", "PB5"]):
        T[n] = nc.alloc_psum_tensor(n, [128, 512], F32)
    T["PT1"] = nc.alloc_psum_tensor("PT1", [128, 1024], BF16)
    T["PT2"] = nc.alloc_psum_tensor("PT2", [128, 1024], BF16)

    sems = {n: nc.alloc_semaphore(n) for n in ["t", "v", "s", "q", "z"]}

    with nc.Block() as block:
        @block.sync
        def _(E):
            emit(P("q", E, sems), T)

        @block.scalar
        def _(E):
            emit(P("s", E, sems), T)

        @block.vector
        def _(E):
            emit(P("v", E, sems), T)

        @block.tensor
        def _(E):
            emit(P("t", E, sems), T)

        @block.gpsimd
        def _(E):
            emit(P("g", E, sems), T)

    return nc


def emit(p, T):
    tk = Toks()
    emit_prep(p, T, tk)
    p.barrier()
    for b in range(NBATCH):
        emit_attn_abc(p, T, tk, b)
        p.barrier()
    for b in range(NBATCH):
        emit_attn_d(p, T, tk, b)
    p.barrier()
    if DEBUG_PHASE != "attn":
        emit_ffn(p, T, tk)
    p.wait_only("g", [("z", p.c["z"]), ("q", p.c["q"]), ("v", p.c["v"]),
                      ("s", p.c["s"]), ("t", p.c["t"])])


def emit_prep(p, T, tk):
    p.op("v", lambda E: E.memset(T["ones_row"][:], 1.0))
    p.op("v", lambda E: E.memset(T["ones_col"][:], 1.0))
    p.op("v", lambda E: E.memset(T["eps_col"][:], EPS))
    p.op("v", lambda E: E.memset(T["ln16_c"][:], float(np.log(1.0 / 16.0))))
    ld = p.dma("q", "q", T["pf32"][:, 0, 0:128], T["ident"])
    p.op("v", lambda E: E.tensor_copy(T["ident_b"][:], T["pf32"][:, 0, 0:128]),
         waits=[("q", ld)], inc=True)
    p.dma("q", "q", T["cost_b"][:], T["costab"])
    p.dma("q", "q", T["sint_b"][:], T["sintab"])

    jobs = []
    for fc in range(FCH):
        jobs.append((T["w1p"][fc].rearrange("p c f -> p (c f)"), T["w1sc"][fc]))
        jobs.append((T["w2p"][fc].rearrange("p h f -> p (h f)"),
                     T["w2sc"][fc].rearrange("p h f -> p (h f)")))
    for cs in range(0, DCH, 4):
        jobs.append((T["wkvd"][:, cs:cs + 4].rearrange("p c f -> p (c f)"),
                     T["kvdsc"][:, cs:cs + 4].rearrange("p c f -> p (c f)")))
    for cs in range(0, DCH, 2):
        jobs.append((T["wqd"][:, cs:cs + 2].rearrange("p c f -> p (c f)"),
                     T["qdsc"][:, cs:cs + 2].rearrange("p c f -> p (c f)")))
    for h in range(H):
        jobs.append((T["wupk"][h].rearrange("p c f -> p (c f)"),
                     T["upksc"][h].rearrange("p c f -> p (c f)")))
        jobs.append((T["wupv"][h].rearrange("p c f -> p (c f)"),
                     T["upvsc"][h].rearrange("p c f -> p (c f)")))
        jobs.append((T["wqup"][h].rearrange("p c f -> p (c f)"),
                     T["qupsc"][h].rearrange("p c f -> p (c f)")))
    for cs in range(0, DCH, 8):
        jobs.append((T["wropes"][:, cs:cs + 8].rearrange("p c f -> p (c f)"),
                     T["ropesc"][:, cs:cs + 8].rearrange("p c f -> p (c f)")))
    for cs in range(DCH):
        jobs.append((T["wout"][:, cs], T["woutsc"][:, cs]))

    last = [0, 0]
    for i, (src, dst) in enumerate(jobs):
        sl = i % 2
        e = "v" if sl == 0 else "s"
        nf = 1
        for d_ in src.shape[1:]:
            nf *= d_
        assert nf <= 2048
        ld = p.dma("q", "q", T["pf32"][:, sl, 0:nf], src, waits=[(e, last[sl])])
        if e == "v":
            cv = p.op("v", lambda E, sl=sl, nf=nf: E.tensor_copy(
                T["pbf"][:, sl, 0:nf], T["pf32"][:, sl, 0:nf]),
                waits=[("q", ld), ("z", max(0, p.c["z"] - 32))], inc=True)
        else:
            cv = p.op("s", lambda E, sl=sl, nf=nf: E.copy(
                T["pbf"][:, sl, 0:nf], T["pf32"][:, sl, 0:nf]),
                waits=[("q", ld), ("z", max(0, p.c["z"] - 32))], inc=True)
        p.dma("s", "z", dst, T["pbf"][:, sl, 0:nf], waits=[(e, cv)])
        last[sl] = cv
    tk["prep_z"] = p.c["z"]


def emit_ln(p, T, src_f32, out_ap, sl, waits=(), out_waits=(),
            out_scale_only=False):
    L = T["lns"][:, sl]
    sv = p.op("s", lambda E: E.activation(T["sqdump"][:], src_f32, AF.Square,
                                          accum_out=L[:, 1:2]),
              waits=waits, inc=True)
    p.op("v", lambda E: E.tensor_reduce(L[:, 0:1], src_f32, AX, OP.add),
         waits=waits)
    p.drain("v")
    p.op("v", lambda E: E.tensor_scalar(L[:, 2:3], L[:, 0:1], 1.0 / D, None,
                                        OP.mult))
    p.op("v", lambda E: E.tensor_scalar(L[:, 3:4], L[:, 1:2], 1.0 / D, None,
                                        OP.mult), waits=[("s", sv)])
    p.drain("v")
    p.op("v", lambda E: E.tensor_mul(L[:, 4:5], L[:, 2:3], L[:, 2:3]))
    p.drain("v")
    vv = p.op("v", lambda E: E.tensor_sub(L[:, 4:5], L[:, 3:4], L[:, 4:5]),
              inc=True)
    sv = p.op("s", lambda E: E.activation(L[:, 7:8], L[:, 4:5], AF.Ln,
                                          bias=T["eps_col"][:]),
              waits=[("v", vv)], inc=True)
    p.drain("s")
    sv = p.op("s", lambda E: E.activation(L[:, 5:6], L[:, 7:8], AF.Exp,
                                          scale=-0.5), inc=True)
    p.op("v", lambda E: E.tensor_mul(L[:, 6:7], L[:, 2:3], L[:, 5:6]),
         waits=[("s", sv)])
    p.drain("v")
    vv = p.op("v", lambda E: E.tensor_scalar(L[:, 6:7], L[:, 6:7], -1.0, None,
                                             OP.mult), inc=True)
    if out_scale_only:
        p.drain("v")
        return vv, sv
    p.drain("v")
    vv = p.op("v", lambda E: E.tensor_scalar(out_ap, src_f32, L[:, 5:6],
                                             L[:, 6:7], OP.mult, OP.add),
              waits=list(out_waits), inc=True)
    return vv, sv


def emit_attn_abc(p, T, tk, b):
    x = T["x"][b]
    PT = [T["PT1"], T["PT2"]]
    ptc = [0]
    pt_cons = [0, 0]
    pt_fill = [0, 0]

    def transpose_block(srcs, dst_fn, waits=(), cwaits=()):
        k = ptc[0] % 2
        ptc[0] += 1
        tt = 0
        for i, s_ in enumerate(srcs):
            w = (list(waits) + [("v", pt_cons[k])]) if i == 0 else []
            tt = p.op("t", lambda E, s=s_, i=i, k=k: E.transpose(
                PT[k][:, i * 128:(i + 1) * 128], s, T["ident_b"][:]),
                waits=w, inc=(i == len(srcs) - 1))
        pt_fill[k] = tt
        vv = p.op("v", lambda E, n=len(srcs), k=k: E.tensor_copy(
            dst_fn(n), PT[k][:, 0:n * 128]),
            waits=[("t", tt)] + list(cwaits), inc=True)
        pt_cons[k] = vv
        return vv, tt

    PB = [T[f"PB{i}"] for i in range(6)]
    pab_cons = [0, 0]
    pcd_cons = [0, 0]
    pcd_scons = [0, 0]
    pef_cons = [0, 0]

    ldrope = p.dma("q", "q", T["rop_b"][:], T["ropesc"][:],
                   waits=[("z", tk["prep_z"])])
    xtl_v = 0
    for it in range(NT):
        sl = it % 2
        ld = p.dma("q", "q", T["xf"][:, sl], x[it * 128:(it + 1) * 128, :],
                   waits=[("v", tk.get(f"xf_free{sl}", 0))])
        lv, _ = emit_ln(p, T, T["xf"][:, sl], T["ln_bf"][:, sl], sl,
                        waits=[("q", ld)],
                        out_waits=[("t", tk.get(f"lnbf_free{sl}", 0))])
        tk.up(f"xf_free{sl}", lv)
        if DBG == "ln":
            p.dma("s", "z", T["dbgb"][b * S + it * 128:b * S + (it + 1) * 128,
                                      0:D], T["ln_bf"][:, sl],
                  waits=[("v", lv)])
        for half in range(2):
            srcs = [T["ln_bf"][:, sl, (half * 8 + c) * 128:(half * 8 + c + 1) * 128]
                    for c in range(8)]
            xtl_v, tt = transpose_block(
                srcs, lambda n, half=half, it=it:
                T["xTl"][:, half * 8:half * 8 + n, it * 128:(it + 1) * 128],
                waits=[("v", lv)])
            tk.up(f"lnbf_free{sl}", tt)
        kq = it % 2
        mm = 0
        for c in range(DCH):
            w = ([("v", xtl_v), ("q", ldrope), ("v", pab_cons[kq])]
                 if c == 0 else [])
            mm = p.op("t", lambda E, c=c, kq=kq, it=it: E.matmul(
                PB[kq][:, 0:256],
                T["xTl"][:, c, it * 128:(it + 1) * 128],
                T["rop_b"][:, c, :], start=(c == 0), stop=(c == DCH - 1)),
                waits=w, inc=(c == DCH - 1))
        ct = T["cost_b"][:, it, :]
        st_ = T["sint_b"][:, it, :]
        tmp = T["pf32"][:, sl]
        stg0 = (it % 2) * 256
        vv = 0
        for half in range(2):
            base = half * 128
            pnv = PB[kq][:, base:base + 128].rearrange("p (a two) -> p two a",
                                                       two=2)
            xe, xo = pnv[:, 0], pnv[:, 1]
            dr = T["sqn"][:, stg0 + half * 128:stg0 + (half + 1) * 128] \
                .rearrange("p (a two) -> p two a", two=2)
            dre, dro = dr[:, 0], dr[:, 1]
            o = half * 256
            w0 = [("t", mm), ("t", tk.get(f"ropestg{it % 2}", 0))]
            p.op("v", lambda E, xe=xe, ct=ct, o=o: E.tensor_mul(
                tmp[:, o:o + 64], xe, ct), waits=w0)
            p.op("v", lambda E, xo=xo, st=st_, o=o: E.tensor_mul(
                tmp[:, o + 64:o + 128], xo, st))
            p.op("v", lambda E, xo=xo, ct=ct, o=o: E.tensor_mul(
                tmp[:, o + 128:o + 192], xo, ct))
            p.op("v", lambda E, xe=xe, st=st_, o=o: E.tensor_mul(
                tmp[:, o + 192:o + 256], xe, st))
            p.drain("v")
            p.op("v", lambda E, dre=dre, o=o: E.tensor_sub(
                dre, tmp[:, o:o + 64], tmp[:, o + 64:o + 128]))
            vv = p.op("v", lambda E, dro=dro, o=o: E.tensor_add(
                dro, tmp[:, o + 128:o + 192], tmp[:, o + 192:o + 256]),
                inc=True)
        pab_cons[kq] = vv
        tk.up(f"xf_free{sl}", vv)
        _, tt = transpose_block(
            [T["sqn"][:, stg0:stg0 + 128]],
            lambda n, it=it: T["krT"][:, it * 128:(it + 1) * 128],
            waits=[("v", vv)])
        _, tt = transpose_block(
            [T["sqn"][:, stg0 + 128:stg0 + 256]],
            lambda n, it=it: T["qrT"][:, it * 128:(it + 1) * 128],
            waits=[("v", vv)])
        tk[f"ropestg{it % 2}"] = tt
    rope_v = p.c["v"]
    if DBG == "rope":
        p.dma("s", "z", T["dbgb"][b * 256:b * 256 + 128, 0:S], T["krT"][:],
              waits=[("v", rope_v)])
        p.dma("s", "z", T["dbgb"][b * 256 + 128:b * 256 + 256, 0:S], T["qrT"][:],
              waits=[("v", rope_v)])

    KRX0 = T["krx"]
    for half, srcb in ((0, T["krT"]), (1, T["qrT"])):
        sq = p.op("v", lambda E, srcb=srcb: E.tensor_mul(
            T["sqn"][:], srcb[:], srcb[:]), waits=[("v", rope_v)], inc=True)
        for ch in range(4):
            k = ch % 2
            mmn = p.op("t", lambda E, ch=ch, k=k: E.matmul(
                PB[2 + k][0:1, 0:512], T["ones_col"][:],
                T["sqn"][:, ch * 512:(ch + 1) * 512], start=True, stop=True),
                waits=[("v", sq), ("v", pcd_cons[k]), ("s", pcd_scons[k])],
                inc=True)
            vv = p.op("v", lambda E, ch=ch, k=k, half=half: E.tensor_reduce(
                KRX0[:, 26 + half * 4 + ch:27 + half * 4 + ch],
                PB[2 + k][0:1, 0:512], AX, OP.max),
                waits=[("t", mmn)], inc=True)
            pcd_cons[k] = vv
        p.drain("v")
        vv = p.op("v", lambda E, half=half: E.tensor_reduce(
            KRX0[:, 24 + half:25 + half],
            KRX0[:, 26 + half * 4:26 + half * 4 + 4], AX, OP.max), inc=True)
        p.drain("v")
    norm_rows_v = p.c["v"]

    for name, scname, ncs, dstb, hoff in [
            ("lat", "kvdsc", CCH, T["latT"], 0),
            ("qlat0", "qdsc", CCH, T["qlatT"], 0),
            ("qlat1", "qdsc", CCH, T["qlatT"], CCH)]:
        if os.environ.get("MK_BAR_B"):
            p.barrier()
        if name == "lat":
            src = T[scname][:]
        else:
            src = T[scname][:, :, hoff * 128:(hoff + CCH) * 128]
        ldw = p.dma("q", "q", T["wdn"][:], src,
                    waits=[("z", tk["prep_z"]), ("t", tk.get("wdn_free", 0))])
        if DBG in ("wdn", "lat") and b == 0:
            widx = {"lat": 0, "qlat0": 1, "qlat1": 2}[name]
            p.dma("s", "z", T["dbgb"][3072 + widx * 128:3072 + (widx + 1) * 128,
                                      0:8192],
                  T["wdn"][:].rearrange("p c f -> p (c f)"),
                  waits=[("q", ldw)])
        for cs in range(ncs):
            for tch in range(4):
                k = (cs * 4 + tch) % 2
                mm = 0
                for c in range(DCH):
                    w = ([("q", ldw), ("v", xtl_v), ("v", pab_cons[k])]
                         if c == 0 else [])
                    mm = p.op("t", lambda E, c=c, cs=cs, tch=tch, k=k:
                              E.matmul(
                                  PB[k][:, 0:512],
                                  T["wdn"][:, c, cs * 128:(cs + 1) * 128],
                                  T["xTl"][:, c, tch * 512:(tch + 1) * 512],
                                  start=(c == 0), stop=(c == DCH - 1)),
                              waits=w, inc=(c == DCH - 1))
                vv = p.op("v", lambda E, cs=cs, tch=tch, k=k, dstb=dstb,
                          hoff=hoff: E.tensor_copy(
                              dstb[:, hoff + cs, tch * 512:(tch + 1) * 512],
                              PB[k][:, 0:512]), waits=[("t", mm)], inc=True)
                pab_cons[k] = vv
        tk["wdn_free"] = p.c["t"]
    lat_v = p.c["v"]
    lat_t = p.c["t"]
    for hs_ in range(2):
        p.op("v", lambda E, hs_=hs_: E.memset(
            T["vh"][:, hs_, :, 128:129], 1.0),
            waits=[("t", lat_t)], inc=True)
    if DBG == "lat":
        p.dma("s", "z", T["dbgb"][b * 384:b * 384 + 128, 0:CCH * S],
              T["latT"][:].rearrange("p c t -> p (c t)"), waits=[("v", lat_v)])
        p.dma("s", "z", T["dbgb"][b * 384 + 128:b * 384 + 256, 0:8192],
              T["qlatT"][:, 0:4].rearrange("p c t -> p (c t)"),
              waits=[("v", lat_v)])
        p.dma("s", "z", T["dbgb"][b * 384 + 256:b * 384 + 384, 0:8192],
              T["qlatT"][:, 4:8].rearrange("p c t -> p (c t)"),
              waits=[("v", lat_v)])

    MS = T["msc"]
    KRX = T["krx"]
    av_t_prev = 0
    exp_done = [0, 0]
    headw_free = [0, 0]
    for h in range(H):
        hs = h % 2
        ldk = p.dma("q", "q", T["wkh"][:, hs], T["upksc"][h],
                    waits=[("z", tk["prep_z"]),
                           ("t", max(headw_free[hs], lat_t))])
        ldv = p.dma("q", "q", T["wvh"][:, hs], T["upvsc"][h])
        ldq = p.dma("q", "q", T["wqh"][:, hs], T["qupsc"][h])
        for tch in range(4):
            k = tch % 2
            mm = 0
            for cc in range(CCH):
                w = ([("q", ldk), ("v", lat_v), ("v", pab_cons[k])]
                     if cc == 0 else [])
                mm = p.op("t", lambda E, cc=cc, tch=tch, k=k, hs=hs: E.matmul(
                    PB[k][:, 0:512], T["wkh"][:, hs, cc, :],
                    T["latT"][:, cc, tch * 512:(tch + 1) * 512],
                    start=(cc == 0), stop=(cc == CCH - 1)),
                    waits=w, inc=(cc == CCH - 1))
            vv = p.op("v", lambda E, tch=tch, k=k, hs=hs: E.tensor_copy(
                T["kh"][:, hs, tch * 512:(tch + 1) * 512], PB[k][:, 0:512]),
                waits=[("t", mm)], inc=True)
            pab_cons[k] = vv
        for tch in range(4):
            k = tch % 2
            mm = 0
            for qc in range(QCH):
                w = ([("q", ldq), ("v", pab_cons[k])] if qc == 0 else [])
                mm = p.op("t", lambda E, qc=qc, tch=tch, k=k, hs=hs: E.matmul(
                    PB[k][:, 0:512], T["wqh"][:, hs, qc, :],
                    T["qlatT"][:, qc, tch * 512:(tch + 1) * 512],
                    start=(qc == 0), stop=(qc == QCH - 1)),
                    waits=w, inc=(qc == QCH - 1))
            vv = p.op("v", lambda E, tch=tch, k=k, hs=hs: E.tensor_copy(
                T["qh"][:, hs, tch * 512:(tch + 1) * 512], PB[k][:, 0:512]),
                waits=[("t", mm)], inc=True)
            pab_cons[k] = vv
        for g4 in range(4):
            k = g4 % 2
            mm = 0
            for j in range(4):
                tt_ = g4 * 4 + j
                for cc in range(CCH):
                    w = ([("q", ldv), ("v", pab_cons[k])]
                         if (j == 0 and cc == 0) else [])
                    mm = p.op("t", lambda E, cc=cc, tt=tt_, j=j, k=k, hs=hs:
                              E.matmul(
                                  PB[k][:, j * 128:(j + 1) * 128],
                                  T["latT"][:, cc, tt * 128:(tt + 1) * 128],
                                  T["wvh"][:, hs, cc, :],
                                  start=(cc == 0), stop=(cc == CCH - 1)),
                              waits=w, inc=(j == 3 and cc == CCH - 1))
            vv = p.op("v", lambda E, g4=g4, k=k, hs=hs: E.tensor_copy(
                T["vh"][:, hs, g4 * 4:(g4 + 1) * 4, 0:128],
                PB[k][:, 0:512].rearrange("p (a f) -> p a f", a=4)),
                waits=[("t", mm)], inc=True)
            pab_cons[k] = vv
        kqv_v = p.c["v"]
        if DBG == "kh":
            r0 = (b * H + h) * 128
            p.dma("s", "z", T["dbgb"][r0:r0 + 128, 0:S], T["kh"][:, hs],
                  waits=[("v", kqv_v)])
            p.dma("s", "z", T["dbgb"][r0:r0 + 128, S:2 * S], T["qh"][:, hs],
                  waits=[("v", kqv_v)])
            p.dma("s", "z", T["dbgb"][r0:r0 + 128, 2 * S:3 * S]
                  .rearrange("p (c t) -> p c t", c=16),
                  T["vh"][:, hs, :, 0:128], waits=[("v", kqv_v)])

        for half, hb in ((0, T["kh"]), (1, T["qh"])):
            sq = p.op("v", lambda E, hb=hb, hs=hs: E.tensor_mul(
                T["sqn"][:], hb[:, hs], hb[:, hs]), waits=[("v", kqv_v)],
                inc=True)
            for ch in range(4):
                k = ch % 2
                mmn = p.op("t", lambda E, ch=ch, k=k: E.matmul(
                    PB[2 + k][0:1, 0:512], T["ones_col"][:],
                    T["sqn"][:, ch * 512:(ch + 1) * 512],
                    start=True, stop=True),
                    waits=[("v", sq), ("v", pcd_cons[k]), ("s", pcd_scons[k])],
                    inc=True)
                vv = p.op("v", lambda E, ch=ch, half=half, k=k: E.tensor_reduce(
                    KRX[:, half * 4 + ch:half * 4 + ch + 1],
                    PB[2 + k][0:1, 0:512], AX, OP.max),
                    waits=[("t", mmn)], inc=True)
                pcd_cons[k] = vv
            p.drain("v")
        vv = p.op("v", lambda E: E.tensor_reduce(
            KRX[:, 14:15], KRX[:, 0:4], AX, OP.max), inc=True)
        p.drain("v")
        vv = p.op("v", lambda E: E.tensor_reduce(
            KRX[:, 15:16], KRX[:, 4:8], AX, OP.max), inc=True)
        p.drain("v")
        vv = p.op("v", lambda E: E.tensor_add(
            KRX[:, 16:17], KRX[:, 14:15], KRX[:, 24:25]),
            waits=[("v", norm_rows_v)], inc=True)
        p.drain("v")
        vv = p.op("v", lambda E: E.tensor_add(
            KRX[:, 17:18], KRX[:, 15:16], KRX[:, 25:26]), inc=True)
        sv = p.op("s", lambda E: E.activation(
            KRX[:, 18:19], KRX[:, 16:17], AF.Ln),
            waits=[("v", vv)], inc=True)
        p.drain("s")
        sv = p.op("s", lambda E: E.activation(
            KRX[:, 19:20], KRX[:, 17:18], AF.Ln), inc=True)
        p.drain("s")
        vv = p.op("v", lambda E: E.tensor_add(
            KRX[:, 20:21], KRX[:, 18:19],
            KRX[:, 19:20]), waits=[("s", sv)], inc=True)
        p.drain("v")
        sv = p.op("s", lambda E: E.activation(
            T["mb"][:], KRX[:, 20:21], AF.Exp, scale=0.5,
            bias=T["ln16_c"][:]), waits=[("v", vv)], inc=True)
        k = h % 2
        mmn = p.op("t", lambda E, k=k: E.matmul(
            PB[2 + k][:, 0:1], T["ones_row"][:], T["mb"][:],
            start=True, stop=True),
            waits=[("s", sv), ("v", pcd_cons[k]), ("s", pcd_scons[k])],
            inc=True)
        negm_v = p.op("v", lambda E, k=k, hs=hs: E.tensor_scalar(
            T["negM"][:, hs:hs + 1], PB[2 + k][:, 0:1], -1.0, None, OP.mult),
            waits=[("t", mmn)], inc=True)
        pcd_cons[k] = negm_v
        if DBG == "mhat":
            r0 = (b * H + h) * 128
            p.dma("s", "z", T["dbgb"][r0:r0 + 128, 0:1],
                  T["negM"][:, hs:hs + 1], waits=[("v", negm_v)])

        for qq in range(4):
            exp_s = 0
            for kc in range(NT):
                k = kc % 2
                w = [("v", kqv_v), ("v", negm_v), ("v", pcd_cons[k]),
                     ("s", pcd_scons[k])]
                p.op("t", lambda E, kc=kc, qq=qq, k=k, hs=hs: E.matmul(
                    PB[2 + k][:, 0:512],
                    T["kh"][:, hs, kc * 128:(kc + 1) * 128],
                    T["qh"][:, hs, qq * 512:(qq + 1) * 512],
                    start=True, stop=False), waits=w)
                mm = p.op("t", lambda E, kc=kc, qq=qq, k=k: E.matmul(
                    PB[2 + k][:, 0:512],
                    T["krT"][:, kc * 128:(kc + 1) * 128],
                    T["qrT"][:, qq * 512:(qq + 1) * 512],
                    start=False, stop=True), inc=True)
                exp_s = p.op("s", lambda E, kc=kc, k=k, hs=hs: E.activation(
                    T["attnT"][:, kc, :], PB[2 + k][:, 0:512], AF.Exp,
                    scale=0.0625, bias=T["negM"][:, hs:hs + 1]),
                    waits=[("t", mm), ("t", av_t_prev)], inc=True)
                pcd_scons[k] = exp_s
            if DBG == "attnT" and h == 0:
                r0 = (b * 4 + qq) * 128
                p.dma("s", "z", T["dbgb"][r0:r0 + 128, 0:8192],
                      T["attnT"][:].rearrange("p c t -> p (c t)"),
                      waits=[("s", exp_s)])
            for qt in range(4):
                k = qt % 2
                mm = 0
                for kc in range(NT):
                    w = ([("s", exp_s), ("v", pef_cons[k])] if kc == 0 else [])
                    mm = p.op("t", lambda E, kc=kc, qt=qt, k=k, hs=hs: E.matmul(
                        PB[4 + k][:, 0:129],
                        T["attnT"][:, kc, qt * 128:(qt + 1) * 128],
                        T["vh"][:, hs, kc, 0:129],
                        start=(kc == 0), stop=(kc == NT - 1)),
                        waits=w, inc=(kc == NT - 1))
                av_t_prev = mm
                p.op("v", lambda E, k=k: E.reciprocal(
                    T["msc"][:, 0, 0:1], PB[4 + k][:, 128:129]),
                    waits=[("t", mm)])
                p.drain("v")
                vv = p.op("v", lambda E, k=k, qt=qt: E.tensor_scalar(
                    T["ln_bf"][:, qt % 2, 0:128], PB[4 + k][:, 0:128],
                    T["msc"][:, 0, 0:1], None, OP.mult),
                    waits=[("t", tk.get(f"lnbf_free{qt % 2}", 0))], inc=True)
                pef_cons[k] = vv
                cv, tt = transpose_block(
                    [T["ln_bf"][:, qt % 2, 0:128]],
                    lambda n, qq=qq, qt=qt: T["ctxs"][:, qq % 2, qt, :],
                    waits=[("v", vv)],
                    cwaits=[("z", tk.get(f"ctxs_free{qq % 2}", 0))])
                tk.up(f"lnbf_free{qt % 2}", tt)
            st = p.dma("s", "z", T["ctxsc"][b, h][:, qq * 512:(qq + 1) * 512],
                       T["ctxs"][:, qq % 2].rearrange("p a f -> p (a f)"),
                       waits=[("v", cv)])
            tk[f"ctxs_free{qq % 2}"] = st
        exp_done[hs] = p.c["s"]
        headw_free[hs] = p.c["t"]
    tk.up("ctx_z", p.c["z"])
    tk["latq_free"] = p.c["t"]


def emit_attn_d(p, T, tk, b):
    x = T["x"][b]
    PB = [T[f"PB{i}"] for i in range(6)]
    pab_scons = [tk.get("projd_s0", 0), tk.get("projd_s1", 0)]
    if b == 0:
        tk["woutb_ld"] = p.dma(
            "q", "q", T["woutb"][:], T["woutsc"][:],
            waits=[("z", tk["prep_z"])])
    for it in range(NT):
        sl = it % 2
        ldc = p.dma("q", "q", T["ctxt"][:, sl],
                    T["ctxsc"][b][:, :, it * 128:(it + 1) * 128]
                    .rearrange("h p f -> p h f"),
                    waits=[("z", tk["ctx_z"]),
                           ("t", tk.get(f"ctxt_free{sl}", 0))])
        ldx = p.dma("q", "q", T["xf"][:, sl], x[it * 128:(it + 1) * 128, :],
                    waits=[("v", tk.get(f"xf_free{sl}", 0))])
        if DBG == "ctx":
            p.dma("s", "z", T["dbgb"][b * S + it * 128:b * S + (it + 1) * 128,
                                      0:2048].rearrange("p (h t) -> p h t", h=16),
                  T["ctxt"][:, sl], waits=[("q", ldc)])
        for dc in range(4):
            k = dc % 2
            mm = 0
            for hh in range(H):
                w = ([("q", ldc), ("q", tk["woutb_ld"]), ("s", pab_scons[k])]
                     if hh == 0 else [])
                mm = p.op("t", lambda E, hh=hh, dc=dc, k=k, sl=sl: E.matmul(
                    PB[k][:, 0:512], T["ctxt"][:, sl, hh, :],
                    T["woutb"][:, hh, dc * 512:(dc + 1) * 512],
                    start=(hh == 0), stop=(hh == H - 1)),
                    waits=w, inc=(hh == H - 1))
            sv = p.op("s", lambda E, dc=dc, k=k, sl=sl: E.copy(
                T["projf"][:, sl, dc * 512:(dc + 1) * 512], PB[k][:, 0:512]),
                waits=[("t", mm), ("v", tk.get(f"projf_free{sl}", 0))],
                inc=True)
            pab_scons[k] = sv
        tk[f"ctxt_free{sl}"] = p.c["t"]
        proj_s = p.c["s"]
        if DBG == "proj":
            p.dma("s", "z", T["dbgb"][b * S + it * 128:b * S + (it + 1) * 128,
                                      0:2 * D].bitcast(F32),
                  T["projf"][:, sl], waits=[("s", proj_s)])
        lv, _ = emit_ln(p, T, T["projf"][:, sl], None, sl,
                        waits=[("s", proj_s)], out_scale_only=True)
        L = T["lns"][:, sl]
        vv = p.op("v", lambda E, sl=sl: E.tensor_scalar(
            T["pf32"][:, sl][:], T["projf"][:, sl], L[:, 5:6], L[:, 6:7],
            OP.mult, OP.add),
            waits=[("z", tk.get(f"pf32_free{sl}", 0))], inc=True)
        tk[f"projf_free{sl}"] = vv
        p.drain("v")
        vv = p.op("v", lambda E, sl=sl: E.tensor_add(
            T["pf32"][:, sl][:], T["pf32"][:, sl][:], T["xf"][:, sl]),
            waits=[("q", ldx)], inc=True)
        tk.up(f"xf_free{sl}", vv)
        row0 = b * S + it * 128
        st = p.dma("s", "z", T["x2sc"][row0:row0 + 128, :], T["pf32"][:, sl][:],
                   waits=[("v", vv)])
        if DEBUG_PHASE == "attn":
            st = p.dma("s", "z", T["out"][b][it * 128:(it + 1) * 128, :],
                       T["pf32"][:, sl][:], waits=[("v", vv)])
        tk[f"pf32_free{sl}"] = st
        tk.up("x2_z", st)
        tk["projd_s0"], tk["projd_s1"] = pab_scons


def emit_ffn(p, T, tk):
    PB = [T[f"PB{i}"] for i in range(6)]
    bank_cons = [("v", 0)] * 6

    def bwait(i):
        return bank_cons[i]

    def bdone(i, eng, tokv):
        bank_cons[i] = (eng, tokv)

    PT = [T["PT1"], T["PT2"]]
    ptc = [0]
    pt_cons = [0, 0]

    res_jobs = []
    for j0 in range(4):
        res_jobs.append((T["b1rf"][:, j0 * 2048:(j0 + 1) * 2048],
                         T["b1b"][:, j0 * 2048:(j0 + 1) * 2048], 2048, ER))
    res_jobs.append((T["b2rf"], T["b2b"][:], 2048, ER))
    res_jobs.append((T["a1t"].rearrange("p c e -> p (c e)"),
                     T["a1b"][:].rearrange("p c e -> p (c e)"), 1024, 128))
    for j0 in range(2):
        res_jobs.append((T["a2t"][:, j0 * 32:(j0 + 1) * 32]
                         .rearrange("p c e -> p (c e)"),
                         T["a2b"][:, j0 * 32:(j0 + 1) * 32]
                         .rearrange("p c e -> p (c e)"), 2048, 128))
    res_jobs.append((T["wrt"].rearrange("p c e -> p (c e)"),
                     T["wrb"][:].rearrange("p c e -> p (c e)"), 128, 128))
    res_jobs.append((T["br"], T["brb"][:], NE, 1))
    res_jobs.append((T["b2"], T["b2row"][:], D, 1))
    last = [0, 0]
    cv = 0
    for i, (src, dst, nf, npart) in enumerate(res_jobs):
        sl = i % 2
        ld = p.dma("q", "q", T["pf32"][0:npart, sl, 0:nf], src,
                   waits=[("v", last[sl])])
        cv = p.op("v", lambda E, sl=sl, nf=nf, npart=npart, dst=dst:
                  E.tensor_copy(dst, T["pf32"][0:npart, sl, 0:nf]),
                  waits=[("q", ld)], inc=True)
        last[sl] = cv
    ldb1t = p.dma("q", "q", T["b1tb"][:], T["b1t"])
    res_v = cv
    res_q = p.c["q"]
    sc_vfree = [0, 0]
    s_store = {}

    for g in range(NGRP):
        xt_v = 0
        for ti in range(4):
            sl = ti % 2
            row0 = g * GT + ti * 128
            ld = p.dma("q", "q", T["xf"][:, sl], T["x2sc"][row0:row0 + 128, :],
                       waits=[("v", tk.get(f"xf_free{sl}", 0)),
                              ("z", tk.get("x2_z", 0))])
            lv, _ = emit_ln(p, T, T["xf"][:, sl], T["ln_bf"][:, sl], sl,
                            waits=[("q", ld), ("v", res_v)],
                            out_waits=[("t", tk.get(f"lnbf_free{sl}", 0))])
            tk.up(f"xf_free{sl}", lv)
            for half in range(2):
                k = ptc[0] % 2
                ptc[0] += 1
                tt = 0
                for c in range(8):
                    src = T["ln_bf"][:, sl, (half * 8 + c) * 128:
                                         (half * 8 + c + 1) * 128]
                    w = ([("v", lv), ("v", pt_cons[k]),
                          ("t", tk.get("xT_free", 0))] if c == 0 else [])
                    tt = p.op("t", lambda E, src=src, c=c, k=k: E.transpose(
                        PT[k][:, c * 128:(c + 1) * 128], src, T["ident_b"][:]),
                        waits=w, inc=(c == 7))
                tk.up(f"lnbf_free{sl}", tt)
                xt_v = p.op("v", lambda E, half=half, ti=ti, k=k: E.tensor_copy(
                    T["xT"][:, half * 8:half * 8 + 8, ti * 128:(ti + 1) * 128],
                    PT[k][:, 0:1024]), waits=[("t", tt)], inc=True)
                pt_cons[k] = xt_v
            mm = 0
            for c in range(DCH):
                w = ([("v", xt_v), ("v", res_v), ("q", res_q), bwait(2)]
                     if c == 0 else [])
                mm = p.op("t", lambda E, c=c, ti=ti: E.matmul(
                    PB[2][:, 0:NE], T["xT"][:, c, ti * 128:(ti + 1) * 128],
                    T["wrb"][:, c, :], start=(c == 0), stop=False), waits=w)
            mm = p.op("t", lambda E: E.matmul(
                PB[2][:, 0:NE], T["ones_row"][:], T["brb"][:],
                start=False, stop=True), inc=True)
            R8 = T["r8"]
            MS = T["msc"][:, sl]
            vv = p.op("v", lambda E: E.tensor_reduce(
                MS[:, 0:1], PB[2][:, 0:NE], AX, OP.max),
                waits=[("t", mm)], inc=True)
            p.drain("v")
            p.op("v", lambda E: E.tensor_scalar(
                T["mk1"][:], PB[2][:, 0:NE], MS[:, 0:1], None, OP.is_equal))
            p.drain("v")
            p.op("v", lambda E: E.scalar_tensor_tensor(
                R8[:, 0:NE], T["mk1"][:], NEG_BIG, PB[2][:, 0:NE],
                OP.mult, OP.add))
            p.drain("v")
            p.op("v", lambda E: E.tensor_reduce(
                MS[:, 1:2], R8[:, 0:NE], AX, OP.max))
            p.drain("v")
            p.op("v", lambda E: E.tensor_scalar(
                T["mk2"][:], R8[:, 0:NE], MS[:, 1:2], None, OP.is_equal))
            vv = p.op("v", lambda E: E.tensor_scalar(
                MS[:, 2:3], MS[:, 0:1], -1.0, None, OP.mult), inc=True)
            sv = p.op("s", lambda E: E.activation(
                R8[:, 8:16], PB[2][:, 0:NE], AF.Exp, bias=MS[:, 2:3]),
                waits=[("v", vv)], inc=True)
            bdone(2, "s", sv)
            vv = p.op("v", lambda E: E.tensor_reduce(
                MS[:, 3:4], R8[:, 8:16], AX, OP.add), waits=[("s", sv)],
                inc=True)
            p.drain("v")
            p.op("v", lambda E, ti=ti: E.reciprocal(
                T["wcol"][:, 0, ti:ti + 1], MS[:, 3:4]))
            p.drain("v")
            sv = p.op("s", lambda E: E.activation(
                MS[:, 4:5], MS[:, 1:2], AF.Exp, bias=MS[:, 2:3]), inc=True)
            vv = p.op("v", lambda E, ti=ti: E.tensor_mul(
                T["wcol"][:, 1, ti:ti + 1], MS[:, 4:5],
                T["wcol"][:, 0, ti:ti + 1]), waits=[("s", sv)], inc=True)
            for ki, mk in ((0, T["mk1"]), (1, T["mk2"])):
                vv = p.op("v", lambda E, mk=mk: E.tensor_copy(
                    T["mk64"][:].rearrange("p (e r) -> p e r", r=NR),
                    mk[:].unsqueeze(2).broadcast_to((128, NE, NR))),
                    waits=[("t", tk.get("mk64_free", 0))], inc=True)
                k = ptc[0] % 2
                ptc[0] += 1
                tt = p.op("t", lambda E, k=k: E.transpose(
                    PT[k][0:ER, 0:128], T["mk64"][:], T["ident_b"][:]),
                    waits=[("v", vv), ("v", pt_cons[k]),
                           ("t", tk.get("maskT_tfree", 0))], inc=True)
                tk["mk64_free"] = tt
                vv = p.op("v", lambda E, ki=ki, ti=ti, k=k: E.tensor_copy(
                    T["maskT"][:, ki, ti * 128:(ti + 1) * 128],
                    PT[k][0:ER, 0:128]), waits=[("t", tt)], inc=True)
                pt_cons[k] = vv
        mask_v = p.c["v"]
        if DBG == "router" and g == 0:
            p.dma("s", "z", T["dbgb"][0:64, 0:512].rearrange(
                "p (k t) -> p k t", k=1) if False else
                T["dbgb"][0:ER, 0:512], T["maskT"][:, 0], waits=[("v", mask_v)])
            p.dma("s", "z", T["dbgb"][64:64 + ER, 0:512], T["maskT"][:, 1],
                  waits=[("v", mask_v)])
            p.dma("s", "z", T["dbgb"][128:256, 0:8].bitcast(F32),
                  T["wcol"][:, 0, :], waits=[("v", mask_v)])
            p.dma("s", "z", T["dbgb"][256:384, 0:8].bitcast(F32),
                  T["wcol"][:, 1, :], waits=[("v", mask_v)])

        mm = 0
        for c in range(DCH):
            w = ([("v", xt_v), bwait(3)] if c == 0 else [])
            mm = p.op("t", lambda E, c=c: E.matmul(
                PB[3][0:ER, 0:512], T["a1b"][:, c, :], T["xT"][:, c, :],
                start=(c == 0), stop=(c == DCH - 1)), waits=w,
                inc=(c == DCH - 1))
        vv = p.op("v", lambda E: E.tensor_copy(T["u1f"][:], PB[3][0:ER, 0:512]),
                  waits=[("t", mm)], inc=True)
        bdone(3, "v", vv)
        p.drain("v")
        z1_v = 0
        for ki in range(2):
            z1_v = p.op("v", lambda E, ki=ki: E.scalar_tensor_tensor(
                T["z1T"][:, ki], T["u1f"][:], 2.0, T["maskT"][:, ki],
                OP.mult, OP.mult),
                waits=[("v", mask_v), ("t", tk.get("z1T_free", 0))], inc=True)

        for fc in range(FCH):
            wsl = fc % 2
            ldw1 = p.dma(
                "q", "q", T["w1c"][:, wsl].rearrange("p c f -> p (c f)"),
                T["w1sc"][fc],
                waits=[("z", tk["prep_z"]), ("t", tk.get(f"w1c_free{wsl}", 0))])
            bk = fc % 2
            mm = 0
            for c in range(DCH):
                w = ([("q", ldw1), ("v", xt_v), bwait(bk)] if c == 0 else [])
                mm = p.op("t", lambda E, c=c, wsl=wsl, bk=bk: E.matmul(
                    PB[bk][:, 0:512], T["w1c"][:, wsl, c, :],
                    T["xT"][:, c, :], start=(c == 0), stop=(c == DCH - 1)),
                    waits=w, inc=(c == DCH - 1))
            tk[f"w1c_free{wsl}"] = mm
            ssl = (fc // 4) % 2
            sv = p.op("s", lambda E, fc=fc, bk=bk, ssl=ssl: E.activation(
                T["sc"][:, ssl, fc % 4, :], PB[bk][:, 0:512], AF.Identity,
                bias=T["b1tb"][:, fc:fc + 1]),
                waits=[("t", mm), ("q", ldb1t),
                       ("v", sc_vfree[ssl]),
                       ("z", s_store.get((g, fc // 4 - 2), 0))],
                inc=True)
            bdone(bk, "s", sv)
            if fc % 4 == 3:
                s_store[(g, fc // 4)] = p.dma(
                    "s", "z",
                    T["ssc"][g, fc - 3:fc + 1].rearrange("f p d -> p f d"),
                    T["sc"][:, ssl])
                if DBG == "s3" and g == 0:
                    r0 = (fc // 4) * 128
                    p.dma("s", "z", T["dbgb"][r0:r0 + 128, 0:2048]
                          .rearrange("p (a f) -> p a f", a=4),
                          T["sc"][:, ssl])

        FH = FCH // 2
        for ki in range(2):
            for fhalf in range(2):
                stage = {}
                DEPTH = 8

                def gelu_stage(fc, stg):
                    cs = fc % 4
                    bk = fc % 2
                    ssl = (fc // 4) % 2
                    cb = fc % 5
                    hslot = fc - fhalf * FH
                    if stg == 0:
                        if cs == 0:
                            stage[("ld", fc // 4)] = p.dma(
                                "q", "q",
                                T["sc"][:, ssl],
                                T["ssc"][g, fc:fc + 4].rearrange(
                                    "f p d -> p f d"),
                                waits=[("z", max(s_store[(g, fc // 4)],
                                                 s_store[(g, 14 + ssl)])),
                                       ("v", sc_vfree[ssl])])
                        stage[("mm", fc)] = p.op(
                            "t", lambda E, fc=fc, bk=bk: E.matmul(
                                PB[bk][:, 0:512],
                                T["b1b"][:, fc * 128:(fc + 1) * 128],
                                T["z1T"][:, ki], start=True, stop=True),
                            waits=[("v", z1_v), bwait(bk)], inc=True)
                    elif stg == 1:
                        tokc = p.op(
                            "v", lambda E, fc=fc, cs=cs, bk=bk, ssl=ssl, cb=cb:
                            E.tensor_add(T["cbuf"][:, cb],
                                         T["sc"][:, ssl, cs, :],
                                         PB[bk][:, 0:512]),
                            waits=[("t", stage[("mm", fc)]),
                                   ("q", stage[("ld", fc // 4)])], inc=True)
                        bdone(bk, "v", tokc)
                        if cs == 3:
                            sc_vfree[ssl] = tokc
                    elif stg == 2:
                        p.op("v", lambda E, fc=fc, cb=cb: E.tensor_mul(
                            T["gq"][:, fc % 2], T["cbuf"][:, cb],
                            T["cbuf"][:, cb]))
                    elif stg == 3:
                        p.op("v", lambda E, fc=fc: E.tensor_scalar(
                            T["gq"][:, fc % 2], T["gq"][:, fc % 2], GCA, GCB,
                            OP.mult, OP.add))
                    elif stg == 4:
                        stage[("z", fc)] = p.op(
                            "v", lambda E, fc=fc, cb=cb: E.tensor_mul(
                                T["gz"][:, fc % 2], T["cbuf"][:, cb],
                                T["gq"][:, fc % 2]), inc=True)
                    elif stg == 5:
                        stage[("g", fc)] = p.op(
                            "s", lambda E, fc=fc: E.activation(
                                T["gz"][:, fc % 2], T["gz"][:, fc % 2],
                                AF.Sigmoid),
                            waits=[("v", stage[("z", fc)])], inc=True)
                    elif stg == 6:
                        stage[("h", fc)] = p.op(
                            "v", lambda E, fc=fc, cb=cb, hslot=hslot:
                            E.tensor_mul(
                                T["hT"][:, hslot, :], T["cbuf"][:, cb],
                                T["gz"][:, fc % 2]),
                            waits=[("s", stage[("g", fc)]),
                                   ("t", tk.get("hT_free", 0))], inc=True)
                    elif stg == 7:
                        stage[("u2", fc)] = p.op(
                            "t", lambda E, fc=fc, hslot=hslot: E.matmul(
                                PB[4][0:ER, 0:512], T["a2b"][:, fc, :],
                                T["hT"][:, hslot, :], start=(fc == 0),
                                stop=(fc == FCH - 1)),
                            waits=([("v", stage[("h", fc)]), bwait(4)]
                                   if fc == 0 else [("v", stage[("h", fc)])]),
                            inc=(fc == FCH - 1))

                for step in range(FH + DEPTH):
                    for stg in range(DEPTH - 1, -1, -1):
                        fc = fhalf * FH + step - stg
                        if fhalf * FH <= fc < (fhalf + 1) * FH:
                            gelu_stage(fc, stg)
                h_v = stage[("h", (fhalf + 1) * FH - 1)]
                if DBG == "hT" and g == 0 and ki == 0:
                    p.dma("s", "z",
                          T["dbgb"][fhalf * 128:(fhalf + 1) * 128, 0:8192]
                          .rearrange("p (c t) -> p c t", c=16),
                          T["hT"][:, 0:16, :], waits=[("v", h_v)])
                    p.dma("s", "z",
                          T["dbgb"][256 + fhalf * 128:256 + (fhalf + 1) * 128,
                                    0:8192].rearrange("p (c t) -> p c t", c=16),
                          T["hT"][:, 16:32, :], waits=[("v", h_v)])
                if fhalf == 1:
                    u2_t = stage[("u2", FCH - 1)]
                    vv = p.op("v", lambda E: E.scalar_tensor_tensor(
                        T["z2T"][:], PB[4][0:ER, 0:512], 2.0, T["maskT"][:, ki],
                        OP.mult, OP.mult), waits=[("t", u2_t)], inc=True)
                    bdone(4, "v", vv)
                    z2_v = vv
                    if DBG == "z2" and g == 0:
                        p.dma("s", "z", T["dbgb"][ki * 64:ki * 64 + ER, 0:512],
                              T["z2T"][:], waits=[("v", z2_v)])

                for dh in range(4):
                    for fcb in range(fhalf * FH, (fhalf + 1) * FH, 4):
                        wsl = (fcb // 4) % 2
                        ldw2 = p.dma(
                            "q", "q",
                            T["w2c"][:, wsl],
                            T["w2sc"][fcb:fcb + 4, :, dh].rearrange(
                                "f p d -> p f d"),
                            waits=[("z", tk["prep_z"]),
                                   ("t", tk.get(f"w2c_free{wsl}", 0))])
                        for j in range(4):
                            fc = fcb + j
                            hslot = fc - fhalf * FH
                            for tsl in range(4):
                                if hslot == 0:
                                    w = [bwait(tsl), ("v", h_v), ("q", ldw2)]
                                elif j == 0 and tsl == 0:
                                    w = [("q", ldw2)]
                                else:
                                    w = []
                                mmw = p.op("t", lambda E, hslot=hslot, j=j,
                                           tsl=tsl, wsl=wsl: E.matmul(
                                               PB[tsl][:, 0:512],
                                               T["hT"][:, hslot,
                                                       tsl * 128:(tsl + 1) * 128],
                                               T["w2c"][:, wsl, j, :],
                                               start=(hslot == 0), stop=False),
                                           waits=w,
                                           inc=(j == 3 and tsl == 3))
                        tk[f"w2c_free{wsl}"] = mmw
                    for tsl in range(4):
                        if fhalf == 0:
                            mm = p.op("t", lambda E, tsl=tsl, dh=dh: E.matmul(
                                PB[tsl][:, 0:512], T["ones_row"][:],
                                T["b2row"][:, dh * 512:(dh + 1) * 512],
                                start=False, stop=True), inc=True)
                        else:
                            mm = p.op("t", lambda E, tsl=tsl, dh=dh: E.matmul(
                                PB[tsl][:, 0:512],
                                T["z2T"][:, tsl * 128:(tsl + 1) * 128],
                                T["b2b"][:, dh * 512:(dh + 1) * 512],
                                start=False, stop=True), waits=[("v", z2_v)],
                                inc=True)
                        first = (ki == 0 and fhalf == 0)
                        wsel = 0 if ki == 0 else 1
                        if first:
                            vv = p.op("v", lambda E, tsl=tsl, dh=dh:
                                      E.tensor_scalar(
                                          T["res"][:, tsl,
                                                   dh * 512:(dh + 1) * 512],
                                          PB[tsl][:, 0:512],
                                          T["wcol"][:, 0, tsl:tsl + 1],
                                          None, OP.mult),
                                      waits=[("t", mm)], inc=True)
                        else:
                            vv = p.op("v", lambda E, tsl=tsl, dh=dh, wsel=wsel:
                                      E.scalar_tensor_tensor(
                                          T["res"][:, tsl,
                                                   dh * 512:(dh + 1) * 512],
                                          PB[tsl][:, 0:512],
                                          T["wcol"][:, wsel, tsl:tsl + 1],
                                          T["res"][:, tsl,
                                                   dh * 512:(dh + 1) * 512],
                                          OP.mult, OP.add),
                                      waits=[("t", mm)], inc=True)
                        bdone(tsl, "v", vv)
                tk["hT_free"] = p.c["t"]
            if DBG == "res0" and g == 0 and ki == 0:
                for tsl_ in range(4):
                    p.dma("s", "z",
                          T["dbgb"][tsl_ * 128:(tsl_ + 1) * 128, 0:4096]
                          .bitcast(F32), T["res"][:, tsl_, :],
                          waits=[("v", p.c["v"])])
            tk["z1T_free"] = p.c["t"]
        res_done_v = p.c["v"]

        if DBG.startswith("res") and g == 0:
            for tsl_ in range(4):
                p.dma("s", "z",
                      T["dbgb"][tsl_ * 128:(tsl_ + 1) * 128, 0:4096]
                      .bitcast(F32), T["res"][:, tsl_, :],
                      waits=[("v", res_done_v)])
        for ti in range(4):
            sl = ti % 2
            row0 = g * GT + ti * 128
            ld = p.dma("q", "q", T["xf"][:, sl], T["x2sc"][row0:row0 + 128, :],
                       waits=[("v", tk.get(f"xf_free{sl}", 0))])
            lv, _ = emit_ln(p, T, T["res"][:, ti, :], None, sl,
                            waits=[("v", res_done_v)], out_scale_only=True)
            L = T["lns"][:, sl]
            vv = p.op("v", lambda E, sl=sl, ti=ti: E.tensor_scalar(
                T["pf32"][:, sl][:], T["res"][:, ti, :], L[:, 5:6], L[:, 6:7],
                OP.mult, OP.add),
                waits=[("z", tk.get(f"pf32_free{sl}", 0))], inc=True)
            p.drain("v")
            vv = p.op("v", lambda E, sl=sl: E.tensor_add(
                T["pf32"][:, sl][:], T["pf32"][:, sl][:], T["xf"][:, sl]),
                waits=[("q", ld)], inc=True)
            tk.up(f"xf_free{sl}", vv)
            bo = row0 // S
            so = row0 % S
            st = p.dma("s", "z", T["out"][bo][so:so + 128, :], T["pf32"][:, sl][:],
                       waits=[("v", vv)])
            tk[f"pf32_free{sl}"] = st
        tk["xT_free"] = p.c["t"]
        tk["maskT_tfree"] = p.c["t"]


def kernel(**inputs):
    x = np.ascontiguousarray(inputs["L0_attn_x"], dtype=np.float32)
    consts = host_prep(inputs)
    nc = build_program()
    in_maps = []
    for core in range(8):
        m = dict(consts)
        m["x"] = x[core * 2:(core + 1) * 2]
        in_maps.append(m)
    res = run_bass_kernel_spmd(nc, in_maps, core_ids=list(range(8)))
    if DBG:
        import ml_dtypes
        return [res.results[i]["dbgb"] for i in range(8)]
    out = np.concatenate([res.results[i]["out"] for i in range(8)], axis=0)
    return out
